# version 06 (final)
# baseline (speedup 1.0000x reference)
import numpy as np
import ml_dtypes
import concourse.bass as bass
import concourse.mybir as mybir
from concourse.bass_utils import run_bass_kernel_spmd

F32 = mybir.dt.float32
BF16 = mybir.dt.bfloat16
I32 = mybir.dt.int32
AF = mybir.ActivationFunctionType
ALU = mybir.AluOpType
AX = mybir.AxisListType
BF = ml_dtypes.bfloat16

B, T, C = 2, 4096, 4096
H, G, D = 32, 8, 128
HPG = H // G
TCORE = 1024
NFC = C // 128
NKS = T // 128
NKT = T // 512
NEG = -float(2 ** 30)
SCALE = 1.0 / float(np.sqrt(D))
EPS_LN = 1e-5



class Sched:
    def __init__(self, nc):
        self.nc = nc
        self.ops = []
        self.res = {}

    def add(self, stream, emit, reads=(), writes=(), dma=False, coll=False, tag=""):
        idx = len(self.ops)
        deps = set()
        for r in reads:
            st = self.res.get(r)
            if st and st["w"] is not None:
                deps.add(st["w"])
        for r in writes:
            st = self.res.get(r)
            if st:
                if st["w"] is not None:
                    deps.add(st["w"])
                deps.update(st["r"])
        for r in reads:
            self.res.setdefault(r, {"w": None, "r": []})["r"].append(idx)
        for r in writes:
            self.res[r] = {"w": idx, "r": []}
        self.ops.append(dict(idx=idx, stream=stream, emit=emit, deps=deps,
                             dma=dma, coll=coll, tag=tag, inc=False,
                             semkey=None, val=None, amt=0))
        return idx

    def dma(self, stream, out, in_, reads=(), writes=(), tag=""):
        if stream == "qa":
            stream = "act"
        def emit(eng):
            return eng.dma_start(out=out, in_=in_)
        return self.add(stream, emit, reads=reads, writes=writes, dma=True, tag=tag)

    def finalize(self):
        ops = self.ops
        for op in ops:
            if op["dma"] or op["coll"]:
                op["inc"] = True
        SELF_SYNC = ("dve", "act", "gp")
        for op in ops:
            for d in op["deps"]:
                p = ops[d]
                if (p["dma"] or p["coll"] or p["stream"] != op["stream"]
                        or p["stream"] in SELF_SYNC):
                    p["inc"] = True
        sem_keys = set()
        lane = {}
        for op in ops:
            if op["inc"]:
                if op["coll"]:
                    op["semkey"] = (op["stream"], "c")
                elif op["dma"]:
                    ln = lane.get(op["stream"], 0)
                    lane[op["stream"]] = (ln + 1) % 8
                    op["semkey"] = (op["stream"], "d%d" % ln)
                else:
                    op["semkey"] = (op["stream"], "e")
                sem_keys.add(op["semkey"])
        self.sem_keys = sorted(sem_keys)
        counts = {k: 0 for k in sem_keys}
        for op in ops:
            if op["inc"]:
                amt = 1 if (op["coll"] or not op["dma"]) else 16
                counts[op["semkey"]] += amt
                op["val"] = counts[op["semkey"]]
                op["amt"] = amt
        self.final_counts = counts
        streams = {}
        for op in ops:
            streams.setdefault(op["stream"], []).append(op)
        self.streams = streams

    def replay(self, stream, eng, sems):
        fl = {}
        for op in self.streams.get(stream, []):
            need = {}
            for d in op["deps"]:
                p = self.ops[d]
                if (p["stream"] == stream and not p["dma"] and not p["coll"]
                        and stream == "pe"):
                    continue
                k = p["semkey"]
                if need.get(k, 0) < p["val"]:
                    need[k] = p["val"]
            for k in sorted(need, key=str):
                if fl.get(k, 0) < need[k]:
                    eng.wait_ge(sems[k], need[k])
                    fl[k] = need[k]
            inst = op["emit"](eng)
            if op["tag"]:
                try:
                    inst.annotate(op["tag"])
                except Exception:
                    pass
            if op["inc"]:
                inst.then_inc(sems[op["semkey"]], op["amt"])



def host_constants():
    identf = np.eye(128, dtype=np.float32)
    pswapA = np.zeros((128, 128), np.float32)
    for i in range(64):
        pswapA[2 * i + 1, 2 * i] = 1.0
        pswapA[2 * i, 2 * i + 1] = 1.0
    pswapB = np.zeros((128, 128), np.float32)
    for i in range(64):
        pswapB[64 + i, i] = 1.0
        pswapB[i, 64 + i] = 1.0
    E_all = np.zeros((128, 16, 128), np.float32)
    for j in range(16):
        for k in range(128):
            E_all[8 * j + k // 16, j, k] = 1.0
    kk = np.arange(128)[:, None]
    qq = np.arange(128)[None, :]
    T0 = np.tile(np.where(kk <= qq, 0.0, NEG), (1, 4)).astype(BF)
    band = np.tile(np.where(qq < kk, 0.0, NEG), (1, 4))
    deny = np.full((128, 512), NEG)
    BANDSEL = np.concatenate([band, deny], axis=1).astype(BF)
    T0Q = np.where(np.arange(128)[:, None] >= np.arange(128)[None, :],
                   0.0, NEG).astype(np.float32)
    DIAG8 = np.where(np.arange(8)[None, :] == (np.arange(128) // 16)[:, None],
                     0.0, NEG).astype(BF)
    T0Qb = np.where(np.arange(128)[:, None] >= np.arange(128)[None, :],
                    0.0, NEG).astype(BF)
    return dict(ident=identf.astype(BF), identf=identf,
                pswapA=pswapA.astype(BF), pswapB=pswapB.astype(BF),
                E_all=np.ascontiguousarray(E_all.reshape(128, 2048)).astype(BF),
                T0=T0, BANDSEL=BANDSEL, T0Qb=T0Qb, DIAG8=DIAG8,
                ones_col=np.ones((128, 1), BF),
                ones_row=np.ones((1, 128), np.float32))


def host_core_tables(core):
    r = core % 4
    base = r * 1024
    pos = (base + np.arange(TCORE)).astype(np.float64)
    freq = 1.0 / (10000.0 ** (np.arange(0, D, 2, dtype=np.float64) / D))
    ang = freq[:, None] * pos[None, :]
    ca, sa = np.cos(ang), np.sin(ang)
    cosA = np.empty((D, TCORE), np.float32); sinA = np.empty((D, TCORE), np.float32)
    cosA[0::2] = ca; cosA[1::2] = ca; sinA[0::2] = -sa; sinA[1::2] = sa
    cosB = np.empty((D, TCORE), np.float32); sinB = np.empty((D, TCORE), np.float32)
    cosB[0:64] = ca; cosB[64:128] = ca; sinB[0:64] = -sa; sinB[64:128] = sa

    cc_cm = np.zeros((128, 8 * 256), np.float32)
    cc_far = np.zeros((128, 8 * 256), np.float32)
    denyA = np.zeros((1, 256), np.float32)
    for qt in range(8):
        gq0 = base + qt * 128
        own0 = gq0 // 16
        q = np.arange(128)[:, None]
        c = np.arange(256)[None, :]
        cc_cm[:, qt * 256:(qt + 1) * 256] = np.where(c < (gq0 + q) // 16, 0.0, NEG)
        cc_far[:, qt * 256:(qt + 1) * 256] = np.where(c < own0, 0.0, NEG) + NEG
        diagks = gq0 // 128
        for ks in range(32):
            denyA[0, qt * 32 + ks] = 0.0 if ks < diagks else -1e9
    regs = np.zeros((1, 64), np.int32)
    kk = np.arange(128)[:, None]
    qq = np.arange(128)[None, :]
    band = np.tile(np.where(qq < kk, 0.0, NEG), (1, 4))
    deny = np.full((128, 512), NEG)
    bandsel4 = np.concatenate(
        [band if (rr == r - 1) else deny for rr in range(4)], axis=1)
    return dict(cosA=cosA, sinA=sinA, cosB=cosB, sinB=sinB,
                cc_cm=cc_cm.astype(BF), cc_far=cc_far.astype(BF),
                denyA=denyA, regs=regs,
                bandsel4=bandsel4.astype(BF))


CONSTS = [
    ("ident", (128, 128), BF16), ("identf", (128, 128), F32),
    ("pswapA", (128, 128), BF16), ("pswapB", (128, 128), BF16),
    ("E_all", (128, 2048), BF16),
    ("T0", (128, 512), BF16), ("BANDSEL", (128, 1024), BF16),
    ("T0Qb", (128, 128), BF16), ("DIAG8", (128, 8), BF16),
    ("ones_col", (128, 1), BF16), ("ones_row", (1, 128), F32),
    ("cc_cm", (128, 2048), BF16), ("cc_far", (128, 2048), BF16),
    ("denyA", (1, 256), F32), ("regs", (1, 64), I32),
]
BANDSEL4_IN = ("bandsel4", (128, 4 * 512), BF16)
ROPE_INS = [("cosA", (D, TCORE), F32), ("sinA", (D, TCORE), F32),
            ("cosB", (D, TCORE), F32), ("sinB", (D, TCORE), F32)]



class KB:
    def __init__(self, dbg=()):
        self.nc = bass.Bass("TRN2", target_bir_lowering=False, debug=False)
        self.sc = Sched(self.nc)
        self._cm = []
        self._regs = {}
        self._regh = {}
        self._declare(dbg)

    def _declare(self, dbg):
        nc = self.nc
        d = {}
        def din(n, shape, dt):
            d[n] = nc.dram_tensor(n, list(shape), dt, kind="ExternalInput").ap()
        din("xd", (C, TCORE), F32)
        din("aWq", (C, H * D), F32); din("aWk", (C, G * D), F32)
        din("aWv", (C, G * D), F32); din("aWo", (H * D, C), F32)
        din("abo", (C,), F32)
        din("bWq", (C, H * D), F32); din("bWk", (C, G * D), F32)
        din("bWv", (C, G * D), F32); din("bWo", (H * D, C), F32)
        din("fW", (C, C), F32); din("fb", (C,), F32)
        din("fA", (C, 32), F32); din("fB", (32, C), F32); din("fL", (8, 32), F32)
        for n, shape, dt in CONSTS:
            din(n, shape, dt)
        din(*BANDSEL4_IN)
        for n, shape, dt in ROPE_INS:
            din(n, shape, dt)
        d["out"] = nc.dram_tensor("out", [C, TCORE], F32, kind="ExternalOutput").ap()
        def dint(n, shape, dt):
            d[n] = nc.dram_tensor(n, list(shape), dt, kind="Internal").ap()
        dint("xdA", (C, TCORE), F32)
        dint("xdB", (C, TCORE), F32)
        dint("qrope", (H * D, TCORE), BF16)
        dint("kvcon", (2048, TCORE), BF16)
        dint("kvgat", (8192, TCORE), BF16)
        dint("ar_in", (1, 128), F32)
        dint("ar_out", (1, 128), F32)
        dint("fWb", (128, 32 * 4096), BF16)
        self.d = d
        self.dbg = {}
        for name, shape, dt in dbg:
            self.dbg[name] = nc.dram_tensor("dbg_" + name, list(shape), dt,
                                            kind="ExternalOutput").ap()

        def sbuf(n, shape, dt):
            cm = nc.sbuf_tensor(n, list(shape), dt)
            self._cm.append(cm)
            return cm.__enter__()
        def psum(n, shape, dt=F32):
            cm = nc.psum_tensor(n, list(shape), dt)
            self._cm.append(cm)
            return cm.__enter__()

        s = {}
        s["big"] = sbuf("big", (128, NFC * TCORE), BF16)
        s["wtile"] = sbuf("wtile", (128, 2 * 4096), BF16)
        s["wtf32"] = sbuf("wtf32", (128, 2048), F32)
        s["xsb"] = sbuf("xsb", (128, 2 * TCORE), F32)
        s["tmpa"] = sbuf("tmpa", (128, 2 * TCORE), F32)
        s["tmpb"] = sbuf("tmpb", (128, 2 * TCORE), F32)
        s["bfa"] = sbuf("bfa", (128, 2 * TCORE), BF16)
        s["sqb"] = sbuf("sqb", (128, 2 * TCORE), BF16)
        s["kg"] = sbuf("kg", (128, T), BF16)
        s["vg"] = sbuf("vg", (128, T), BF16)
        s["qg"] = sbuf("qg", (128, HPG * TCORE), BF16)
        s["pt"] = sbuf("pt", (128, 4 * 512), BF16)
        s["cm"] = sbuf("cm", (128, HPG * 264), BF16)
        s["ge01"] = sbuf("ge01", (128, 264), BF16)
        s["bias_q"] = sbuf("bias_q", (128, HPG * 256), BF16)
        s["biasT"] = sbuf("biasT", (128, 2 * 2 * 512), BF16)
        s["biastab"] = sbuf("biastab", (128, 256), F32)
        s["nb8T"] = sbuf("nb8T", (128, 2 * 512), BF16)
        s["bsb"] = sbuf("bsb", (128, 512), BF16)
        rows_t = sbuf("rows_t", (128, 3072), F32)
        s["rowsc"] = rows_t[0:1, :]
        s["rows2"] = rows_t[32:33, :]
        s["rows3"] = rows_t[64:65, :]
        s["colsc"] = sbuf("colsc", (128, 16), F32)
        s["g1"] = sbuf("g1", (128, 64), F32)
        s["g2"] = sbuf("g2", (128, 64), F32)
        s["g3"] = sbuf("g3", (128, 64), F32)
        s["gb"] = sbuf("gb", (128, 64), BF16)
        s["smb"] = sbuf("smb", (128, 64), BF16)
        s["fbst"] = sbuf("fbst", (32, 128), F32)
        s["fbc"] = sbuf("fbc", (32, 128), BF16)
        s["asb"] = sbuf("asb", (128, NFC * 32), BF16)
        s["regs"] = sbuf("regs_sb", (1, 64), I32)
        s["c_bs4"] = sbuf("c_bs4", (128, 512), BF16)
        s["c_cos"] = sbuf("c_cos", (D, TCORE), F32)
        s["c_sin"] = sbuf("c_sin", (D, TCORE), F32)
        s["dyn"] = sbuf("dyn", (128, 2 * 512), BF16)
        s["bnd"] = sbuf("bnd", (128, 4 * 256), BF16)
        for n, shape, dt in CONSTS:
            if n != "regs":
                s["c_" + n] = sbuf("c_" + n, shape, dt)
        self.s = s

        p = {}
        p["W0"] = psum("psW0", (128, 512))
        p["W1"] = psum("psW1", (128, 512))
        p["X0"] = psum("psX0", (128, 512))
        p["S0"] = psum("psS0", (128, 512))
        p["S1"] = psum("psS1", (128, 512))
        p["OD"] = psum("psOD", (128, 512))
        p["M"] = psum("psM", (128, 512))
        p["TB"] = psum("psTB", (128, 1024), BF16)
        self.p = p

    def pe(self, emit, rd, wr, tag=""):
        return self.sc.add("pe", emit, rd, wr, tag=tag)

    def act(self, emit, rd, wr, tag=""):
        return self.sc.add("act", emit, rd, wr, tag=tag)

    def dve(self, emit, rd, wr, tag=""):
        return self.sc.add("dve", emit, rd, wr, tag=tag)

    def gpc(self, emit, rd, wr, tag=""):
        return self.sc.add("gp", emit, rd, wr, tag=tag)

    def mm(self, out, lhsT, rhs, start, stop, rd, wr, tag=""):
        def emit(eng, out=out, lhsT=lhsT, rhs=rhs, start=start, stop=stop):
            return eng.matmul(out, lhsT, rhs, start=start, stop=stop)
        return self.pe(emit, rd, wr, tag=tag)

    def transp(self, out, in_, rd, wr, tag=""):
        ident = self.s["c_identf" if in_.dtype == F32 else "c_ident"]
        ires = "c_identf" if in_.dtype == F32 else "c_ident"
        n = in_.shape[0]
        def emit(eng, out=out, in_=in_, ident=ident, n=n):
            return eng.transpose(out, in_, ident[0:n, 0:n])
        return self.pe(emit, list(rd) + [ires], wr, tag=tag)

    def copy(self, eng, out, in_, rd, wr, tag=""):
        if eng == "act":
            def emit(e, out=out, in_=in_):
                return e.copy(out=out, in_=in_)
        else:
            def emit(e, out=out, in_=in_):
                return e.tensor_copy(out=out, in_=in_)
        return self.sc.add(eng, emit, rd, wr, tag=tag)

    def rload(self, stream, name, cell, mx=3968, tag=""):
        s = self.s
        import bass_rust
        def emit(eng, name=name, cell=cell, stream=stream, mx=mx):
            key = (stream, name)
            if key not in self._regh:
                self._regh[key] = eng.alloc_register(name=f"r_{stream}_{name}")
            reg = self._regh[key]
            inst = eng.reg_load(reg, s["regs"][0:1, cell:cell + 1])
            self._regs[key] = bass_rust.RuntimeValue(reg, 0, mx)
            return inst
        return self.sc.add(stream, emit, ["regs_sb"], [], tag=tag)

    def reg(self, stream, name):
        return self._regs[(stream, name)]

    def load_consts(self):
        sc, s, d = self.sc, self.s, self.d
        for n, shape, dt in CONSTS:
            tgt = s["regs"] if n == "regs" else s["c_" + n]
            res = "regs_sb" if n == "regs" else "c_" + n
            sc.dma("qs", tgt[:], d[n], reads=(), writes=[res], tag=f"ldc_{n}")
        src = d["fA"].rearrange("(fc p) r -> p fc r", p=128)
        for hh in range(2):
            stage = s["wtf32"][:, hh * 512:(hh + 1) * 512]
            sc.dma("qs", stage.rearrange("p (fc r) -> p fc r", r=32),
                   src[:, hh * 16:(hh + 1) * 16, :], reads=(),
                   writes=[f"wtf32.{hh}"], tag="ldA")
            self.copy("dve", s["asb"][:, hh * 512:(hh + 1) * 512], stage,
                      [f"wtf32.{hh}"], ["asb"], tag="castA")
        sc.dma("qs", s["fbst"][0:8, 0:32], d["fL"], reads=(), writes=["fbst"],
               tag="ldL")
        self.copy("dve", s["smb"][0:8, 0:32], s["fbst"][0:8, 0:32], ["fbst"],
                  ["smb.L"], tag="castL")
        self.transp(self.p["TB"][0:32, 0:8], s["smb"][0:8, 0:32], ["smb.L"],
                    ["pTB"], tag="trL")
        self.copy("act", s["smb"][0:32, 32:40], self.p["TB"][0:32, 0:8], ["pTB"],
                  ["smb.LT"], tag="evLT")

    def ln(self, src_name, tag):
        sc, s, p = self.sc, self.s, self.p
        src = self.d[src_name]
        ones = s["c_ones_col"]
        for fc in range(NFC):
            bi = fc % 2
            xs = s["xsb"][:, bi * 1024:(bi + 1) * 1024]
            xb = s["bfa"][:, bi * 1024:(bi + 1) * 1024]
            sq = s["sqb"][:, bi * 1024:(bi + 1) * 1024]
            sc.dma("qs", xs, src[fc * 128:(fc + 1) * 128, :],
                   reads=[f"{src_name}.{fc}"], writes=[f"xsb.{bi}"],
                   tag=f"{tag}ldx{fc}")
            self.copy("dve", xb, xs, [f"xsb.{bi}"], [f"bfa.{bi}"], tag=f"{tag}cst{fc}")
            def c_sq(eng, xs=xs, sq=sq):
                return eng.activation(sq, xs, AF.Square)
            self.act(c_sq, [f"xsb.{bi}"], [f"sqb.{bi}"], tag=f"{tag}sq{fc}")
            st, sp = fc == 0, fc == NFC - 1
            for hh in range(2):
                self.mm(p["W0" if hh == 0 else "W1"][0:1, :], ones[:],
                        s["bfa"][:, bi * 1024 + hh * 512:bi * 1024 + (hh + 1) * 512],
                        st, sp, [f"bfa.{bi}", "c_ones_col"], [f"pW{hh}"],
                        tag=f"{tag}s{hh}_{fc}")
                self.mm(p["S0" if hh == 0 else "S1"][0:1, :], ones[:],
                        s["sqb"][:, bi * 1024 + hh * 512:bi * 1024 + (hh + 1) * 512],
                        st, sp, [f"sqb.{bi}", "c_ones_col"], [f"pS{hh}"],
                        tag=f"{tag}q{hh}_{fc}")
        rs, r2, r3 = s["rowsc"], s["rows2"], s["rows3"]
        inv_n = 1.0 / C
        for hh in range(2):
            def c_m(eng, hh=hh):
                return eng.tensor_scalar(out=rs[0:1, 2048 + hh * 512:2048 + (hh + 1) * 512],
                                         in0=p["W0" if hh == 0 else "W1"][0:1, :],
                                         scalar1=inv_n, scalar2=None, op0=ALU.mult)
            self.dve(c_m, [f"pW{hh}"], [f"lnm.{hh}"], tag=f"{tag}m{hh}")
            def c_q(eng, hh=hh):
                return eng.tensor_scalar(out=r2[0:1, hh * 512:(hh + 1) * 512],
                                         in0=p["S0" if hh == 0 else "S1"][0:1, :],
                                         scalar1=inv_n, scalar2=None, op0=ALU.mult)
            self.dve(c_q, [f"pS{hh}"], [f"lnq.{hh}"], tag=f"{tag}v{hh}")
        def c_v1(eng):
            return eng.tensor_tensor(out=r2[0:1, 1024:2048], in0=rs[0:1, 2048:3072],
                                     in1=rs[0:1, 2048:3072], op=ALU.mult)
        self.dve(c_v1, ["lnm.0", "lnm.1"], ["lnv"], tag=f"{tag}v1")
        def c_v2(eng):
            return eng.tensor_tensor(out=r2[0:1, 1024:2048], in0=r2[0:1, 0:1024],
                                     in1=r2[0:1, 1024:2048], op=ALU.subtract)
        self.dve(c_v2, ["lnq.0", "lnq.1", "lnv"], ["lnv"], tag=f"{tag}v2")
        def c_v3(eng):
            return eng.tensor_scalar(out=r2[0:1, 1024:2048], in0=r2[0:1, 1024:2048],
                                     scalar1=EPS_LN, scalar2=None, op0=ALU.add)
        self.dve(c_v3, ["lnv"], ["lnv"], tag=f"{tag}v3")
        def c_sd(eng):
            return eng.activation(r3[0:1, 0:1024], r2[0:1, 1024:2048], AF.Sqrt)
        self.act(c_sd, ["lnv"], ["lnsd"], tag=f"{tag}sd")
        def c_r(eng):
            return eng.reciprocal(out=rs[0:1, 0:1024], in_=r3[0:1, 0:1024])
        self.dve(c_r, ["lnsd", "lnq.0", "lnq.1"], ["lnr"], tag=f"{tag}r")
        def c_a(eng):
            return eng.scalar_tensor_tensor(out=rs[0:1, 1024:2048],
                                            in0=rs[0:1, 2048:3072], scalar=-1.0,
                                            in1=rs[0:1, 0:1024],
                                            op0=ALU.mult, op1=ALU.mult)
        self.dve(c_a, ["lnm.0", "lnm.1", "lnr", "lnsd"], ["lna"], tag=f"{tag}a")
        orow = s["c_ones_row"]
        for pp, row, rres, wres in [
                (p["W0"], rs[0:1, 0:512], "lnr", "pW0"),
                (p["W1"], rs[0:1, 512:1024], "lnr", "pW1"),
                (p["S0"], rs[0:1, 1024:1536], "lna", "pS0"),
                (p["S1"], rs[0:1, 1536:2048], "lna", "pS1")]:
            self.mm(pp[:], orow[:], row, True, True, [rres, "c_ones_row"], [wres],
                    tag=f"{tag}bc")
        for fc in range(NFC):
            bi = fc % 2
            xs = s["xsb"][:, bi * 1024:(bi + 1) * 1024]
            ta = s["tmpa"][:, bi * 1024:(bi + 1) * 1024]
            sc.dma("gp", xs, src[fc * 128:(fc + 1) * 128, :],
                   reads=[f"{src_name}.{fc}"], writes=[f"xsb.{bi}"],
                   tag=f"{tag}ld2_{fc}")
            for hh in range(2):
                rb = p["W0"] if hh == 0 else p["W1"]
                ab = p["S0"] if hh == 0 else p["S1"]
                def c_n1(eng, xs=xs, ta=ta, rb=rb, hh=hh):
                    return eng.tensor_tensor(out=ta[:, hh * 512:(hh + 1) * 512],
                                             in0=xs[:, hh * 512:(hh + 1) * 512],
                                             in1=rb[:], op=ALU.mult)
                self.dve(c_n1, [f"xsb.{bi}", f"pW{hh}"], [f"tmpa.{bi}.{hh}"],
                         tag=f"{tag}n1_{fc}{hh}")
                def c_n2(eng, ta=ta, ab=ab, fc=fc, hh=hh):
                    return eng.tensor_tensor(
                        out=self.s["big"][:, fc * 1024 + hh * 512:fc * 1024 + (hh + 1) * 512],
                        in0=ta[:, hh * 512:(hh + 1) * 512], in1=ab[:], op=ALU.add)
                self.dve(c_n2, [f"tmpa.{bi}.{hh}", f"pS{hh}"], [f"big.{fc}"],
                         tag=f"{tag}n2_{fc}{hh}")

    def wcol_load(self, wname, col0, owidth, buf, tag):
        sc, s = self.sc, self.s
        src = self.d[wname][:, col0:col0 + owidth].rearrange(
            "(fc p) o -> p fc o", p=128)
        per = max(1, 512 // owidth)
        wres = f"wtile.{buf}"
        queues = ["qs", "qa", "gp"]
        for pi in range((NFC + per - 1) // per):
            f0, f1 = pi * per, min((pi + 1) * per, NFC)
            fb = pi % 4
            stage = s["wtf32"][:, fb * 512:fb * 512 + (f1 - f0) * owidth]
            sc.dma(queues[pi % 3], stage.rearrange("p (fc o) -> p fc o", o=owidth),
                   src[:, f0:f1, :], reads=[], writes=[f"wtf32.{fb}"],
                   tag=f"{tag}wl{pi}")
            dst = s["wtile"][:, buf * 4096 + f0 * owidth:buf * 4096 + f1 * owidth]
            self.copy("dve", dst, stage, [f"wtf32.{fb}"], [wres], tag=f"{tag}wc{pi}")
        return wres

    def qkv(self, layer, tag):
        sc, s, p, d = self.sc, self.s, self.p, self.d
        pre = "a" if layer == "a" else "b"
        sc.dma("qs", s["c_cos"][:], d["cosA" if layer == "a" else "cosB"],
               reads=[], writes=["c_cos"], tag=f"{tag}ldcos")
        sc.dma("qs", s["c_sin"][:], d["sinA" if layer == "a" else "sinB"],
               reads=[], writes=["c_sin"], tag=f"{tag}ldsin")
        cos, sin = s["c_cos"], s["c_sin"]
        cosr, sinr = "c_cos", "c_sin"
        psw = s["c_pswapA" if layer == "a" else "c_pswapB"]
        pswr = "c_pswapA" if layer == "a" else "c_pswapB"

        def proj(col0, wname, kind, idx):
            buf = idx % 2
            wres = self.wcol_load(wname, col0, 128, buf, f"{tag}{kind}{idx}")
            for hh in range(2):
                pw = p["W0" if hh == 0 else "W1"]
                for fc in range(NFC):
                    lhs = s["wtile"][:, buf * 4096 + fc * 128:buf * 4096 + (fc + 1) * 128]
                    rhs = s["big"][:, fc * 1024 + hh * 512:fc * 1024 + (hh + 1) * 512]
                    self.mm(pw[:], lhs, rhs, fc == 0, fc == NFC - 1,
                            [wres, f"big.{fc}"], [f"pW{hh}"], tag=f"{tag}{kind}mm")
            bb = s["bfa"][:, buf * 1024:(buf + 1) * 1024]
            for hh in range(2):
                pw = p["W0" if hh == 0 else "W1"]
                def c_ev(eng, pw=pw, bb=bb, hh=hh):
                    return eng.copy(out=bb[:, hh * 512:(hh + 1) * 512], in_=pw[:])
                self.act(c_ev, [f"pW{hh}"], [f"bfa.{buf}"], tag=f"{tag}{kind}ev")
            if kind in ("q", "k"):
                ro = s["sqb"][:, buf * 1024:(buf + 1) * 1024]
                for hh in range(2):
                    pw = p["W0" if hh == 0 else "W1"]
                    self.mm(p["X0"][:], psw[:], bb[:, hh * 512:(hh + 1) * 512],
                            True, True, [f"bfa.{buf}", pswr], ["pX0"],
                            tag=f"{tag}{kind}sw")
                    ta = s["tmpa"][:, buf * 1024 + hh * 512:buf * 1024 + (hh + 1) * 512]
                    tb = s["tmpb"][:, buf * 1024 + hh * 512:buf * 1024 + (hh + 1) * 512]
                    def c_r1(eng, ta=ta, hh=hh):
                        return eng.tensor_tensor(
                            out=ta, in0=p["X0"][:],
                            in1=sin[:, hh * 512:(hh + 1) * 512], op=ALU.mult)
                    self.dve(c_r1, ["pX0", sinr], [f"tmpa.{buf}.{hh}"],
                             tag=f"{tag}{kind}r1")
                    def c_r2(eng, pw=pw, tb=tb, hh=hh):
                        return eng.tensor_tensor(
                            out=tb, in0=pw[:],
                            in1=cos[:, hh * 512:(hh + 1) * 512], op=ALU.mult)
                    self.dve(c_r2, [f"pW{hh}", cosr], [f"tmpb.{buf}.{hh}"],
                             tag=f"{tag}{kind}r2")
                    def c_r3(eng, ta=ta, tb=tb, ro=ro, hh=hh):
                        return eng.tensor_tensor(out=ro[:, hh * 512:(hh + 1) * 512],
                                                 in0=ta, in1=tb, op=ALU.add)
                    self.dve(c_r3, [f"tmpa.{buf}.{hh}", f"tmpb.{buf}.{hh}"],
                             [f"sqb.{buf}"], tag=f"{tag}{kind}r3")
                if kind == "q":
                    dst, dres = d["qrope"][idx * 128:(idx + 1) * 128, :], "qrope"
                else:
                    dst, dres = d["kvcon"][idx * 128:(idx + 1) * 128, :], "kvconK"
                sc.dma("qa", dst, ro, reads=[f"sqb.{buf}"], writes=[dres],
                       tag=f"{tag}{kind}st{idx}")
                sq2 = s["bfa"][:, buf * 1024:(buf + 1) * 1024]
                def c_sq(eng, ro=ro, sq2=sq2):
                    return eng.activation(sq2, ro, AF.Square)
                self.act(c_sq, [f"sqb.{buf}"], [f"bfa.{buf}"], tag=f"{tag}{kind}sq")
                pn = p["M"] if kind == "q" else p["OD"]
                pres = "pM" if kind == "q" else "pOD"
                base = 128 if kind == "q" else 192
                for hh in range(2):
                    self.mm(pn[0:1, :], s["c_ones_col"][:],
                            sq2[:, hh * 512:(hh + 1) * 512], True, True,
                            [f"bfa.{buf}", "c_ones_col"], [pres],
                            tag=f"{tag}{kind}nm")
                    col = base + idx * 2 + hh
                    def c_rd(eng, pn=pn, col=col):
                        return eng.tensor_reduce(out=s["rows2"][0:1, col:col + 1],
                                                 in_=pn[0:1, :], axis=AX.X,
                                                 op=ALU.max)
                    self.dve(c_rd, [pres], [f"nrm.{col}"], tag=f"{tag}{kind}nr")
            else:
                for t8 in range(8):
                    self.transp(p["TB"][:, 0:128], bb[:, t8 * 128:(t8 + 1) * 128],
                                [f"bfa.{buf}"], ["pTB"], tag=f"{tag}vtr")
                    cidx = (t8 * 8 + idx) * 128
                    vbuf = s["kg"] if cidx < 4096 else s["vg"]
                    dst = vbuf[:, cidx % 4096:cidx % 4096 + 128]
                    vres0 = "kg.0" if cidx < 4096 else "vg.0"
                    self.copy("act", dst, p["TB"][:, 0:128], ["pTB"],
                              [f"vgst.{t8}.{idx}", vres0], tag=f"{tag}vcp")

        for hq in range(H):
            proj(hq * 128, pre + "Wq", "q", hq)
        for gk in range(G):
            proj(gk * 128, pre + "Wk", "k", gk)
        for gv in range(G):
            proj(gv * 128, pre + "Wv", "v", gv)
        vdst = d["kvcon"][1024:2048, :].rearrange("(tb p) (g dd) -> p tb g dd",
                                                  p=128, g=8)
        for half in range(2):
            vsrc = (s["kg"] if half == 0 else s["vg"])[:, 0:4096].rearrange(
                "p (tb g dd) -> p tb g dd", tb=4, g=8)
            sc.dma("qa", vdst[:, half * 4:(half + 1) * 4, :, :], vsrc,
                   reads=[f"vgst.{t8}.{i}" for t8 in range(half * 4, half * 4 + 4)
                          for i in range(8)] + ["kg.0" if half == 0 else "vg.0"],
                   writes=["kvconV", "kg.0" if half == 0 else "vg.0"],
                   tag=f"{tag}vstore{half}")
        def c_mq(eng):
            return eng.tensor_reduce(out=s["rows2"][0:1, 0:1],
                                     in_=s["rows2"][0:1, 128:192], axis=AX.X,
                                     op=ALU.max)
        self.dve(c_mq, [f"nrm.{128 + i}" for i in range(64)], ["mxq"], tag=f"{tag}mq")
        def c_mk(eng):
            return eng.tensor_reduce(out=s["rows2"][0:1, 1:2],
                                     in_=s["rows2"][0:1, 192:208], axis=AX.X,
                                     op=ALU.max)
        self.dve(c_mk, [f"nrm.{192 + i}" for i in range(16)], ["mxk"], tag=f"{tag}mk")
        def c_m2(eng):
            return eng.tensor_tensor(out=s["rows2"][0:1, 2:3],
                                     in0=s["rows2"][0:1, 0:1],
                                     in1=s["rows2"][0:1, 1:2], op=ALU.mult)
        self.dve(c_m2, ["mxq", "mxk"], ["m2x"], tag=f"{tag}m2")
        def c_sq2(eng):
            return eng.activation(s["rows2"][0:1, 3:4], s["rows2"][0:1, 2:3], AF.Sqrt)
        self.act(c_sq2, ["m2x"], ["msc"], tag=f"{tag}ms")
        def c_ng(eng):
            return eng.tensor_scalar(out=s["rowsc"][0:1, 512:513],
                                     in0=s["rows2"][0:1, 3:4],
                                     scalar1=-SCALE * 1.02, scalar2=None,
                                     op0=ALU.mult)
        self.dve(c_ng, ["msc"], ["negsM1"], tag=f"{tag}mn")
        self.mm(p["X0"][:, 0:1], s["c_ones_row"][:], s["rowsc"][0:1, 512:513],
                True, True, ["negsM1", "c_ones_row"], ["pX0"], tag=f"{tag}mb")
        self.copy("act", s["colsc"][:, 0:1], p["X0"][:, 0:1], ["pX0"], ["negsM"],
                  tag=f"{tag}mbe")

    def gather(self, tag):
        d = self.d
        def c_ag(eng):
            return eng.collective_compute(
                "AllGather", ALU.bypass,
                replica_groups=[[0, 1, 2, 3], [4, 5, 6, 7]],
                ins=[d["kvcon"].opt()], outs=[d["kvgat"].opt()])
        self.sc.add("gp", c_ag, reads=["kvconK", "kvconV"], writes=["kvgat"],
                    coll=True, tag=f"{tag}ag")

    def attn(self, layer, tag):
        sc, s, d = self.sc, self.s, self.d
        p = self.p
        self._pend_fin = None
        if layer == "a":
            self.mm(p["X0"][:, 0:256], s["c_ones_row"][:], d and s["c_denyA"][0:1, :],
                    True, True, ["c_denyA", "c_ones_row"], ["pX0"], tag=f"{tag}btb")
            def c_btb(eng):
                return eng.activation(s["biastab"][:], p["X0"][:, 0:256],
                                      AF.Identity, bias=s["colsc"][:, 0:1],
                                      scale=1.0)
            self.act(c_btb, ["pX0", "negsM"], ["biastab"], tag=f"{tag}btbe")
        for g in range(G):
            gb = 0
            for rr in range(4):
                sc.dma("qs", s["kg"][:, rr * 1024:(rr + 1) * 1024],
                       d["kvgat"][rr * 2048 + g * 128:rr * 2048 + (g + 1) * 128, :],
                       reads=["kvgat"], writes=["kg.0"], tag=f"{tag}kld{g}")
                vsrc = d["kvgat"][rr * 2048 + 1024:rr * 2048 + 2048,
                                  g * 128:(g + 1) * 128]
                sc.dma("qs",
                       s["vg"][:, rr * 1024:(rr + 1) * 1024]
                       .rearrange("p (j dd) -> p j dd", j=8),
                       vsrc.rearrange("(j p) dd -> p j dd", p=128),
                       reads=["kvgat"], writes=["vg.0"], tag=f"{tag}vld{g}")
            for hh in range(HPG):
                sc.dma("qs", s["qg"][:, hh * 1024:(hh + 1) * 1024],
                       d["qrope"][(g * HPG + hh) * 128:(g * HPG + hh + 1) * 128, :],
                       reads=["qrope"], writes=["qg.0"], tag=f"{tag}qld{g}")
            kgv = s["kg"][:]
            vgv = s["vg"][:]
            qgv = s["qg"][:]
            for qt in range(8):
                self._pend_fin = self._attn_qt(layer, tag, g, qt, gb, kgv, vgv,
                                               qgv, self._pend_fin)
        if self._pend_fin is not None:
            self._pend_fin()
            self._pend_fin = None

    def _attn_qt(self, layer, tag, g, qt, gb, kgv, vgv, qgv, pend_fin=None):
        sc, s, p, d = self.sc, self.s, self.p, self.d
        isb = layer == "b"
        t = f"{tag}g{g}q{qt}"
        kres, vres, qres = "kg.0", "vg.0", "qg.0"
        qp = qt % 2
        qrhs = qgv.rearrange("p (h tt) -> p h tt", h=HPG)[:, :, qt * 128:(qt + 1) * 128]
        biasT = s["biasT"][:, qp * 1024:(qp + 1) * 1024]
        nb8T = s["nb8T"][:, qp * 512:(qp + 1) * 512]
        dyn = s["dyn"][:, qp * 512:(qp + 1) * 512]
        bTr = lambda half: f"bT.{qp}.{half}"
        nbr = f"nb8T.{qp}"
        dynr = lambda i: f"dyn.{qp}.{i}"

        sc.dma("qs", dyn[:, 0:128],
               d["kvcon"][g * 128:(g + 1) * 128, qt * 128:(qt + 1) * 128],
               reads=["kvconK"], writes=[dynr(0)], tag=f"{t}sgk")
        sc.dma("qs", dyn[:, 128:256],
               d["kvcon"][1024 + qt * 128:1024 + (qt + 1) * 128,
                          g * 128:(g + 1) * 128],
               reads=["kvconV"], writes=[dynr(1)], tag=f"{t}sgv")
        if isb and qt >= 1:
            sc.dma("qs", dyn[:, 256:384],
                   d["kvcon"][g * 128:(g + 1) * 128, (qt - 1) * 128:qt * 128],
                   reads=["kvconK"], writes=[dynr(2)], tag=f"{t}sbk")
            sc.dma("qs", dyn[:, 384:512],
                   d["kvcon"][1024 + (qt - 1) * 128:1024 + qt * 128,
                              g * 128:(g + 1) * 128],
                   reads=["kvconV"], writes=[dynr(3)], tag=f"{t}sbv")
        if isb and qt == 0:
            for rr in range(4):
                sc.dma("qs", s["bnd"][:, rr * 256:rr * 256 + 128],
                       d["kvgat"][rr * 2048 + g * 128:rr * 2048 + (g + 1) * 128,
                                  896:1024],
                       reads=["kvgat"], writes=[f"bnd.{rr}"], tag=f"{t}c{rr}k")
                sc.dma("qs", s["bnd"][:, rr * 256 + 128:(rr + 1) * 256],
                       d["kvgat"][rr * 2048 + 1024 + 896:rr * 2048 + 2048,
                                  g * 128:(g + 1) * 128],
                       reads=["kvgat"], writes=[f"bnd.{rr}"], tag=f"{t}c{rr}v")

        if isb:
            for hh in range(HPG):
                qtile = qgv[:, hh * 1024 + qt * 128:hh * 1024 + (qt + 1) * 128]
                c0 = hh * 264
                for kt in range(NKT):
                    ps = p["W0" if kt % 2 == 0 else "W1"]
                    psr = "pW0" if kt % 2 == 0 else "pW1"
                    self.mm(ps[:], qtile, kgv[:, kt * 512:(kt + 1) * 512],
                            True, True, [qres, kres], [psr], tag=f"{t}h{hh}qk{kt}")
                    def c_red(eng, ps=ps, c0=c0, kt=kt):
                        return eng.tensor_reduce(
                            out=s["cm"][:, c0 + kt * 32:c0 + (kt + 1) * 32],
                            in_=ps[:].rearrange("p (c w) -> p c w", w=16),
                            axis=AX.X, op=ALU.max)
                    self.dve(c_red, [psr], [f"cm.{hh}"], tag=f"{t}h{hh}cr{kt}")
                def c_ccm(eng, c0=c0, qt=qt):
                    return eng.tensor_tensor(
                        out=s["cm"][:, c0:c0 + 256],
                        in0=s["cm"][:, c0:c0 + 256],
                        in1=s["c_cc_cm"][:, qt * 256:(qt + 1) * 256], op=ALU.add)
                self.gpc(c_ccm, [f"cm.{hh}", "c_cc_cm"], [f"cm.{hh}"],
                         tag=f"{t}h{hh}cc")
                def c_qkd(eng, qtile=qtile):
                    return eng.matmul(p["X0"][:, 0:128], qtile, dyn[:, 0:128],
                                      start=True, stop=False)
                self.pe(c_qkd, [qres, dynr(0)], ["pX0"], tag=f"{t}h{hh}dqk")
                def c_qkdm(eng):
                    return eng.matmul(p["X0"][:, 0:128], s["c_ident"][:],
                                      s["c_T0Qb"][:], start=False, stop=True)
                self.pe(c_qkdm, ["c_T0Qb", "c_ident"], ["pX0"], tag=f"{t}h{hh}dqm")
                def c_dred(eng, c0=c0):
                    return eng.tensor_reduce(
                        out=s["cm"][:, c0 + 256:c0 + 264],
                        in_=p["X0"][:, 0:128].rearrange("p (c w) -> p c w", w=16),
                        axis=AX.X, op=ALU.max)
                self.dve(c_dred, ["pX0"], [f"cm.{hh}"], tag=f"{t}h{hh}dr")
                def c_d8(eng, c0=c0):
                    return eng.tensor_tensor(
                        out=s["cm"][:, c0 + 256:c0 + 264],
                        in0=s["cm"][:, c0 + 256:c0 + 264],
                        in1=s["c_DIAG8"][:], op=ALU.add)
                self.gpc(c_d8, [f"cm.{hh}", "c_DIAG8"], [f"cm.{hh}"],
                         tag=f"{t}h{hh}d8")
                def c_m8(eng, c0=c0):
                    return eng.max(out=s["g2"][:, 0:8], in_=s["cm"][:, c0:c0 + 264])
                self.dve(c_m8, [f"cm.{hh}"], ["g2"], tag=f"{t}h{hh}m8")
                def c_ge(eng, c0=c0):
                    return eng.tensor_scalar(out=s["ge01"][:, 0:256],
                                             in0=s["cm"][:, c0:c0 + 256],
                                             scalar1=s["g2"][:, 7:8], scalar2=None,
                                             op0=ALU.is_ge)
                self.dve(c_ge, [f"cm.{hh}", "g2"], ["ge01"], tag=f"{t}h{hh}ge")
                def c_t8s(eng, hh=hh):
                    return eng.tensor_copy(out=s["g3"][:, 56 + hh:57 + hh],
                                           in_=s["g2"][:, 7:8])
                self.gpc(c_t8s, ["g2"], ["g3t8"], tag=f"{t}h{hh}t8s")
                def c_gi(eng):
                    return eng.memset(s["ge01"][:, 0:8], 1.0)
                self.gpc(c_gi, [], ["ge01"], tag=f"{t}h{hh}gi")
                def c_bq(eng, hh=hh, qt=qt):
                    return eng.scalar_tensor_tensor(
                        out=s["bias_q"][:, hh * 256:(hh + 1) * 256],
                        in0=s["ge01"][:, 0:256], scalar=-NEG,
                        in1=s["c_cc_far"][:, qt * 256:(qt + 1) * 256],
                        op0=ALU.mult, op1=ALU.add)
                self.dve(c_bq, ["ge01", "c_cc_far"], [f"bq.{hh}"],
                         tag=f"{t}h{hh}bq")
                for half in range(2):
                    self.transp(p["TB"][:, 0:128],
                                s["bias_q"][:, hh * 256 + half * 128:hh * 256 + (half + 1) * 128],
                                [f"bq.{hh}"], ["pTB"], tag=f"{t}h{hh}tr{half}")
                    self.copy("act",
                              biasT[:, half * 512 + hh * 128:half * 512 + (hh + 1) * 128],
                              p["TB"][:, 0:128], ["pTB"], [bTr(half)],
                              tag=f"{t}h{hh}te{half}")
            def c_tm(eng):
                return eng.tensor_reduce(out=s["colsc"][:, 8:9],
                                         in_=s["cm"][:, 0:HPG * 264],
                                         axis=AX.X, op=ALU.max)
            self.dve(c_tm, [f"cm.{hh2}" for hh2 in range(HPG)], ["nmt1"],
                     tag=f"{t}tm")
            self.transp(p["X0"][0:1, 0:128], s["colsc"][:, 8:9], ["nmt1"],
                        ["pX0"], tag=f"{t}tmt")
            def c_tm2(eng):
                return eng.tensor_reduce(out=s["rowsc"][0:1, 520:521],
                                         in_=p["X0"][0:1, 0:128], axis=AX.X,
                                         op=ALU.max)
            self.dve(c_tm2, ["pX0"], ["nmt2"], tag=f"{t}tm2")
            def c_tm3(eng):
                return eng.tensor_scalar(out=s["rowsc"][0:1, 521:522],
                                         in0=s["rowsc"][0:1, 520:521],
                                         scalar1=-SCALE, scalar2=None,
                                         op0=ALU.mult)
            self.dve(c_tm3, ["nmt2"], ["nmt3"], tag=f"{t}tm3")
            self.mm(p["X0"][:, 0:1], s["c_ones_row"][:], s["rowsc"][0:1, 521:522],
                    True, True, ["nmt3", "c_ones_row"], ["pX0"], tag=f"{t}tmb")
            self.copy("act", s["colsc"][:, 10 + qp:11 + qp], p["X0"][:, 0:1],
                      ["pX0"], [f"negsMt.{qp}"], tag=f"{t}tme")

        def build_nb8T(ktile_ap, ktile_res, nbtag):
            for hh in range(HPG):
                qtile = qgv[:, hh * 1024 + qt * 128:hh * 1024 + (qt + 1) * 128]
                self.mm(p["X0"][:, 0:128], qtile, ktile_ap, True, True,
                        [qres, ktile_res], ["pX0"], tag=f"{nbtag}h{hh}bqk")
                def c_bred(eng):
                    return eng.tensor_reduce(
                        out=s["gb"][:, 8:16],
                        in_=p["X0"][:, 0:128].rearrange("p (c w) -> p c w", w=16),
                        axis=AX.X, op=ALU.max)
                self.dve(c_bred, ["pX0"], ["gb"], tag=f"{nbtag}h{hh}br")
                def c_bge(eng, hh=hh):
                    return eng.tensor_scalar(out=s["g1"][:, 8:16],
                                             in0=s["gb"][:, 8:16],
                                             scalar1=s["g3"][:, 56 + hh:57 + hh],
                                             scalar2=None, op0=ALU.is_ge)
                self.dve(c_bge, ["gb", "g3t8"], ["g1"], tag=f"{nbtag}h{hh}bge")
                def c_bnb(eng):
                    return eng.tensor_scalar(out=s["gb"][:, 0:8],
                                             in0=s["g1"][:, 8:16],
                                             scalar1=NEG, scalar2=None,
                                             op0=ALU.mult)
                self.dve(c_bnb, ["g1"], ["gb"], tag=f"{nbtag}h{hh}bnb")
                self.transp(p["TB"][0:8, 128:256], s["gb"][:, 0:8],
                            ["gb"], ["pTB"], tag=f"{nbtag}h{hh}ntr")
                self.copy("act", nb8T[0:8, hh * 128:(hh + 1) * 128],
                          p["TB"][0:8, 128:256], ["pTB"], [nbr],
                          tag=f"{nbtag}h{hh}nte")

        if pend_fin is not None:
            pend_fin()
        ident = s["c_ident"]
        if isb and qt == 0:
            units = ["cand0", "cand1", "cand2", "cand3", "diag"] + list(range(NKS))
        elif isb:
            units = ["band", "diag"] + list(range(NKS))
        else:
            units = ["diag"] + list(range(NKS))
        nunits = len(units)
        pend = []

        def flush_one(first, last):
            ptv, ptr, pvk, pvkr = pend.pop(0)
            self.mm(p["M"][0:1, :], s["c_ones_col"][:], ptv, first, last,
                    [ptr, "c_ones_col"], ["pM"], tag=f"{t}sg")
            def c_pv(eng, ptv=ptv, pvk=pvk, first=first, last=last):
                return eng.matmul(p["OD"][:], pvk, ptv, start=first, stop=last)
            self.pe(c_pv, [pvkr, ptr], ["pOD"], tag=f"{t}pv")

        for pos_u, u in enumerate(units):
            stp = p["S0"] if pos_u % 2 == 0 else p["S1"]
            stpr = "pS0" if pos_u % 2 == 0 else "pS1"
            if u == "band" or (isinstance(u, str) and u.startswith("cand")):
                if u == "band":
                    ktile, vtile = dyn[:, 256:384], dyn[:, 384:512]
                    ktr, vtr = dynr(2), dynr(3)
                    bmask = s["c_BANDSEL"][:, 0:512]
                    bmr = "c_BANDSEL"
                else:
                    rr = int(u[4])
                    ktile = s["bnd"][:, rr * 256:rr * 256 + 128]
                    vtile = s["bnd"][:, rr * 256 + 128:(rr + 1) * 256]
                    ktr = vtr = f"bnd.{rr}"
                    sc.dma("qs", s["c_bs4"][:],
                           d["bandsel4"][:, rr * 512:(rr + 1) * 512],
                           reads=[], writes=["c_bs4"], tag=f"{t}bs4{rr}")
                    bmask = s["c_bs4"][:]
                    bmr = "c_bs4"
                build_nb8T(ktile, ktr, f"{t}nb{pos_u}")
                def c_qkb(eng, stp=stp, ktile=ktile):
                    return eng.matmul(stp[:], ktile, qrhs, start=True, stop=False)
                self.pe(c_qkb, [ktr, qres], [stpr], tag=f"{t}bqk{pos_u}")
                def c_bmask(eng, stp=stp, bmask=bmask):
                    return eng.matmul(stp[:], ident[:], bmask,
                                      start=False, stop=False)
                self.pe(c_bmask, [bmr, "c_ident"], [stpr], tag=f"{t}bm{pos_u}")
                def c_bnb2(eng, stp=stp):
                    return eng.matmul(stp[:], s["c_E_all"][:, 0:128],
                                      nb8T[:, 0:512], start=False, stop=True)
                self.pe(c_bnb2, ["c_E_all", nbr], [stpr], tag=f"{t}bnb{pos_u}")
                pvk, pvkr = vtile, vtr
            elif u == "diag":
                def c_qkd2(eng, stp=stp):
                    return eng.matmul(stp[:], dyn[:, 0:128], qrhs,
                                      start=True, stop=False)
                self.pe(c_qkd2, [dynr(0), qres], [stpr], tag=f"{t}dqk2")
                def c_dmask(eng, stp=stp):
                    return eng.matmul(stp[:], ident[:], s["c_T0"][:],
                                      start=False, stop=True)
                self.pe(c_dmask, ["c_T0", "c_ident"], [stpr], tag=f"{t}dm2")
                pvk, pvkr = dyn[:, 128:256], dynr(1)
            else:
                ks = u
                half, j = ks // 16, ks % 16
                if isb:
                    self.mm(stp[:], kgv[:, ks * 128:(ks + 1) * 128], qrhs,
                            True, False, [kres, qres], [stpr], tag=f"{t}f{ks}qk")
                    self.mm(stp[:], s["c_E_all"][:, j * 128:(j + 1) * 128],
                            biasT[:, half * 512:(half + 1) * 512], False, True,
                            ["c_E_all", bTr(half)], [stpr], tag=f"{t}f{ks}m")
                else:
                    self.mm(stp[:], kgv[:, ks * 128:(ks + 1) * 128], qrhs,
                            True, True, [kres, qres], [stpr], tag=f"{t}f{ks}qk")
                pvk, pvkr = vgv[:, ks * 128:(ks + 1) * 128], vres
            ptv = s["pt"][:, (pos_u % 4) * 512:((pos_u % 4) + 1) * 512]
            ptr = f"pt.{pos_u % 4}"
            if isb:
                bcol, bres = s["colsc"][:, 10 + qp:11 + qp], f"negsMt.{qp}"
            elif u == "diag":
                bcol, bres = s["colsc"][:, 0:1], "negsM"
            else:
                col = qt * 32 + u
                bcol, bres = s["biastab"][:, col:col + 1], "biastab"
            def c_exp(eng, stp=stp, ptv=ptv, bcol=bcol):
                return eng.activation(ptv, stp[:], AF.Exp, bias=bcol, scale=SCALE)
            self.act(c_exp, [stpr, bres], [ptr], tag=f"{t}e{pos_u}")
            pend.append((ptv, ptr, pvk, pvkr))
            if len(pend) > 3:
                flush_one(pos_u == 3, False)
        while pend:
            flush_one(nunits <= 4 and len(pend) == nunits,
                      len(pend) == 1)
        def finalize():
            def c_rc(eng):
                return eng.reciprocal(out=s["rowsc"][0:1, 0:512], in_=p["M"][0:1, :])
            self.dve(c_rc, ["pM"], ["invs"], tag=f"{t}rc")
            self.mm(p["M"][:], s["c_ones_row"][:], s["rowsc"][0:1, 0:512],
                    True, True, ["invs", "c_ones_row"], ["pM"], tag=f"{t}bcs")
            def c_bce(eng):
                return eng.copy(out=s["tmpb"][:, 0:512], in_=p["M"][:])
            self.act(c_bce, ["pM"], ["tmpb.0.0"], tag=f"{t}bce")
            def c_ods(eng, g=g, qt=qt):
                out = s["big"][:, 0:NFC * TCORE].rearrange(
                    "p (h tt) -> p h tt", h=H)[:, g * HPG:(g + 1) * HPG,
                                               qt * 128:(qt + 1) * 128]
                return eng.tensor_tensor(
                    out=out,
                    in0=p["OD"][:].rearrange("p (h tt) -> p h tt", h=HPG),
                    in1=s["tmpb"][:, 0:512].rearrange("p (h tt) -> p h tt", h=HPG),
                    op=ALU.mult)
            wr = [f"big.{g * HPG + hh}" for hh in range(HPG)]
            self.dve(c_ods, ["pOD", "tmpb.0.0"], wr, tag=f"{t}ods")
        return finalize


    def wo(self, layer, src_name, dst_name, tag):
        sc, s, p, d = self.sc, self.s, self.p, self.d
        wname = ("a" if layer == "a" else "b") + "Wo"
        src, dst = d[src_name], d[dst_name]
        for oc in range(NFC):
            buf = oc % 2
            wres = self.wcol_load(wname, oc * 128, 128, buf, f"{tag}o{oc}")
            for hd in range(H):
                lhs = s["wtile"][:, buf * 4096 + hd * 128:buf * 4096 + (hd + 1) * 128]
                for hh in range(2):
                    rhs = s["big"][:, 0:NFC * TCORE].rearrange(
                        "p (h tt) -> p h tt", h=H)[:, hd, hh * 512:(hh + 1) * 512]
                    self.mm(p["W0" if hh == 0 else "W1"][:], lhs, rhs,
                            hd == 0, hd == H - 1, [wres, f"big.{hd}"],
                            [f"pW{hh}"], tag=f"{tag}mm{oc}")
            if layer == "a":
                sc.dma("qs", s["colsc"][:, 2:3],
                       d["abo"][oc * 128:(oc + 1) * 128].rearrange("(pp o) -> pp o", o=1),
                       reads=[], writes=["bo"], tag=f"{tag}bo{oc}")
            bi = oc % 2
            xs = s["xsb"][:, bi * 1024:(bi + 1) * 1024]
            sc.dma("gp", xs, src[oc * 128:(oc + 1) * 128, :],
                   reads=[f"{src_name}.{oc}"], writes=[f"xsb.{bi}"],
                   tag=f"{tag}ldr{oc}")
            for hh in range(2):
                pw = p["W0" if hh == 0 else "W1"]
                ta = s["tmpa"][:, bi * 1024 + hh * 512:bi * 1024 + (hh + 1) * 512]
                if layer == "a":
                    def c_ev(eng, pw=pw, ta=ta):
                        return eng.activation(ta, pw[:], AF.Identity,
                                              bias=s["colsc"][:, 2:3], scale=1.0)
                    self.act(c_ev, [f"pW{hh}", "bo"], [f"tmpa.{bi}.{hh}"],
                             tag=f"{tag}ev{oc}{hh}")
                else:
                    def c_ev(eng, pw=pw, ta=ta):
                        return eng.copy(out=ta, in_=pw[:])
                    self.act(c_ev, [f"pW{hh}"], [f"tmpa.{bi}.{hh}"],
                             tag=f"{tag}ev{oc}{hh}")
                tb = s["tmpb"][:, bi * 1024 + hh * 512:bi * 1024 + (hh + 1) * 512]
                def c_add(eng, ta=ta, xs=xs, tb=tb, hh=hh):
                    return eng.tensor_tensor(out=tb, in0=ta,
                                             in1=xs[:, hh * 512:(hh + 1) * 512],
                                             op=ALU.add)
                self.dve(c_add, [f"tmpa.{bi}.{hh}", f"xsb.{bi}"],
                         [f"tmpb.{bi}.{hh}"], tag=f"{tag}ad{oc}{hh}")
            sc.dma("qs", dst[oc * 128:(oc + 1) * 128, :],
                   s["tmpb"][:, bi * 1024:(bi + 1) * 1024],
                   reads=[f"tmpb.{bi}.0", f"tmpb.{bi}.1"],
                   writes=[dst_name, f"{dst_name}.{oc}"], tag=f"{tag}st{oc}")

    def ffn(self, src_name, dst_name, tag, wcache="store"):
        sc, s, p, d = self.sc, self.s, self.p, self.d
        src, dst = d[src_name], d[dst_name]
        for hh in range(2):
            for fc in range(NFC):
                lhs = s["asb"][:, fc * 32:(fc + 1) * 32]
                rhs = s["big"][:, fc * 1024 + hh * 512:fc * 1024 + (hh + 1) * 512]
                self.mm(p["X0"][0:32, :], lhs, rhs, fc == 0, fc == NFC - 1,
                        ["asb", f"big.{fc}"], ["pX0"], tag=f"{tag}zmm")
            def c_ze(eng, hh=hh):
                return eng.copy(out=s["bfa"][0:32, hh * 512:(hh + 1) * 512],
                                in_=p["X0"][0:32, :])
            self.act(c_ze, ["pX0"], ["zd"], tag=f"{tag}ze{hh}")
        if wcache == "load":
            sc.dma("qs", s["wtile"][:, 0:4096], d["fWb"][:, 0:4096],
                   reads=["fWb.0"], writes=["wtile.0"], tag=f"{tag}h0wl")
            wres = "wtile.0"
        else:
            wres = self.wcol_load("fW", 0, 128, 0, f"{tag}h0")
        sc.dma("qs", s["colsc"][:, 3:4],
               d["fb"][0:128].rearrange("(pp o) -> pp o", o=1), reads=[],
               writes=["fb0"], tag=f"{tag}fb0")
        for hh in range(2):
            pw = p["W0" if hh == 0 else "W1"]
            for fc in range(NFC):
                lhs = s["wtile"][:, fc * 128:(fc + 1) * 128]
                rhs = s["big"][:, fc * 1024 + hh * 512:fc * 1024 + (hh + 1) * 512]
                self.mm(pw[:], lhs, rhs, fc == 0, fc == NFC - 1,
                        [wres, f"big.{fc}"], [f"pW{hh}"], tag=f"{tag}h0mm")
            def c_h0(eng, pw=pw, hh=hh):
                return eng.activation(s["tmpa"][:, hh * 512:(hh + 1) * 512], pw[:],
                                      AF.Identity, bias=s["colsc"][:, 3:4], scale=1.0)
            self.act(c_h0, [f"pW{hh}", "fb0"], [f"tmpa.0.{hh}"], tag=f"{tag}h0e{hh}")
        for at in range(4):
            hs = s["tmpa"][0:8, at * 256 + 1:(at + 1) * 256:2]
            self.transp(p["M"][:, 0:8], hs, ["tmpa.0.0", "tmpa.0.1"], ["pM"],
                        tag=f"{tag}hst{at}")
            self.copy("act", s["g1"][:, at * 8:(at + 1) * 8], p["M"][:, 0:8],
                      ["pM"], ["g1"], tag=f"{tag}hse{at}")
        zanch = s["bfa"][0:32, 1:1024:2]
        self.mm(p["M"][0:8, :], s["smb"][0:32, 32:40], zanch, True, True,
                ["smb.LT", "zd"], ["pM"], tag=f"{tag}ds")
        def c_dse(eng):
            return eng.copy(out=s["tmpb"][0:8, 0:512], in_=p["M"][0:8, :])
        self.act(c_dse, ["pM"], ["tmpb.0.0"], tag=f"{tag}dse")
        for at in range(4):
            self.transp(p["M"][:, 0:8], s["tmpb"][0:8, at * 128:(at + 1) * 128],
                        ["tmpb.0.0"], ["pM"], tag=f"{tag}dst{at}")
            self.copy("act", s["g1"][:, 32 + at * 8:32 + (at + 1) * 8],
                      p["M"][:, 0:8], ["pM"], ["g1"], tag=f"{tag}dte{at}")
        self.emit_dbg("f_zd", self.s["bfa"][0:32, 0:1024], ["zd"])
        self.emit_dbg("f_g1", self.s["g1"][:], ["g1"])
        def c_hm(eng):
            return eng.tensor_reduce(out=s["colsc"][:, 4:5], in_=s["g1"][:, 0:32],
                                     axis=AX.X, op=ALU.max, apply_absolute_value=True)
        self.dve(c_hm, ["g1"], ["hm1"], tag=f"{tag}hm")
        def c_dm(eng):
            return eng.tensor_reduce(out=s["colsc"][:, 5:6], in_=s["g1"][:, 32:64],
                                     axis=AX.X, op=ALU.max, apply_absolute_value=True)
        self.dve(c_dm, ["g1"], ["dm1"], tag=f"{tag}dm")
        self.transp(p["M"][0:1, 0:128], s["colsc"][:, 4:5], ["hm1"], ["pM"],
                    tag=f"{tag}hmt")
        def c_hm2(eng):
            return eng.tensor_reduce(out=s["rows3"][0:1, 0:1],
                                     in_=p["M"][0:1, 0:128], axis=AX.X, op=ALU.max)
        self.dve(c_hm2, ["pM"], ["hm2"], tag=f"{tag}hm2")
        self.transp(p["M"][0:1, 0:128], s["colsc"][:, 5:6], ["dm1"], ["pM"],
                    tag=f"{tag}dmt")
        def c_dm2(eng):
            return eng.tensor_reduce(out=s["rows3"][0:1, 1:2],
                                     in_=p["M"][0:1, 0:128], axis=AX.X, op=ALU.max)
        self.dve(c_dm2, ["pM"], ["dm2"], tag=f"{tag}dm2")
        def c_pk(eng):
            return eng.memset(s["rows3"][0:1, 2:128], 0.0)
        self.dve(c_pk, [], ["arpad"], tag=f"{tag}pk")
        self.emit_dbg("f_loc", s["rows3"][0:1, 0:2], ["hm2", "dm2"])
        sc.dma("qs", d["ar_in"], s["rows3"][0:1, 0:128],
               reads=["hm2", "dm2", "arpad"], writes=["ar_in"], tag=f"{tag}arw")
        def c_ar(eng):
            return eng.collective_compute(
                "AllReduce", ALU.max, replica_groups=[[0, 1, 2, 3, 4, 5, 6, 7]],
                ins=[d["ar_in"].opt()], outs=[d["ar_out"].opt()])
        self.sc.add("gp", c_ar, reads=["ar_in"], writes=["ar_out"], coll=True,
                    tag=f"{tag}ar")
        sc.dma("qs", s["rows2"][0:1, 208:336], d["ar_out"], reads=["ar_out"],
               writes=["gsc0"], tag=f"{tag}arr")
        def c_c1(eng):
            return eng.tensor_scalar(out=s["rows2"][0:1, 336:338],
                                     in0=s["rows2"][0:1, 208:210], scalar1=1e-6,
                                     scalar2=None, op0=ALU.max)
        self.dve(c_c1, ["gsc0"], ["gsc1"], tag=f"{tag}c1")
        def c_c2(eng):
            return eng.reciprocal(out=s["rows2"][0:1, 338:340],
                                  in_=s["rows2"][0:1, 336:338])
        self.dve(c_c2, ["gsc1"], ["gsc2"], tag=f"{tag}c2")
        def c_c3(eng):
            return eng.tensor_scalar(out=s["rowsc"][0:1, 514:516],
                                     in0=s["rows2"][0:1, 338:340], scalar1=0.5,
                                     scalar2=None, op0=ALU.mult)
        self.dve(c_c3, ["gsc2"], ["gsc3"], tag=f"{tag}c3")
        self.mm(p["M"][:, 0:2], s["c_ones_row"][:], s["rowsc"][0:1, 514:516],
                True, True, ["gsc3", "c_ones_row"], ["pM"], tag=f"{tag}cbb")
        self.copy("act", s["colsc"][:, 6:8], p["M"][:, 0:2], ["pM"], ["c12"],
                  tag=f"{tag}cbe")
        self.emit_dbg("f_sc", self.s["rows2"][0:1, 208:336], ["gsc0"])
        self.emit_dbg("f_r2", self.s["rows2"][0:1, 336:340], ["gsc2"])
        self.emit_dbg("f_rc", self.s["rowsc"][0:1, 512:518], ["gsc3"])
        self.emit_dbg("f_c12", self.s["colsc"][:, 0:16], ["c12"])
        for at in range(4):
            ht = s["g1"][:, at * 8:(at + 1) * 8]
            dt = s["g1"][:, 32 + at * 8:32 + (at + 1) * 8]
            def c_l1(eng, ht=ht):
                return eng.tensor_scalar(out=s["g2"][:, 8:16], in0=ht,
                                         scalar1=s["colsc"][:, 6:7], scalar2=None,
                                         op0=ALU.mult)
            self.dve(c_l1, ["g1", "c12"], ["g2"], tag=f"{tag}l1_{at}")
            def c_l2(eng, dt=dt):
                return eng.scalar_tensor_tensor(out=s["g2"][:, 0:8], in0=dt,
                                                scalar=s["colsc"][:, 7:8],
                                                in1=s["g2"][:, 8:16],
                                                op0=ALU.mult, op1=ALU.add)
            self.dve(c_l2, ["g1", "c12", "g2"], ["g2"], tag=f"{tag}l2_{at}")
            def c_mx(eng):
                return eng.tensor_reduce(out=s["g2"][:, 16:17], in_=s["g2"][:, 0:8],
                                         axis=AX.X, op=ALU.max)
            self.dve(c_mx, ["g2"], ["g2"], tag=f"{tag}mx{at}")
            def c_nm(eng):
                return eng.tensor_scalar(out=s["g2"][:, 17:18], in0=s["g2"][:, 16:17],
                                         scalar1=-1.0, scalar2=None, op0=ALU.mult)
            self.dve(c_nm, ["g2"], ["g2"], tag=f"{tag}nm{at}")
            def c_ex(eng):
                return eng.activation(s["g2"][:, 24:32], s["g2"][:, 0:8], AF.Exp,
                                      bias=s["g2"][:, 17:18], scale=1.0,
                                      accum_out=s["g2"][:, 18:19])
            self.act(c_ex, ["g2"], ["g2"], tag=f"{tag}ex{at}")
            def c_rc2(eng):
                return eng.reciprocal(out=s["g2"][:, 19:20], in_=s["g2"][:, 18:19])
            self.dve(c_rc2, ["g2"], ["g2"], tag=f"{tag}rc{at}")
            def c_pr(eng):
                return eng.tensor_scalar(out=s["g3"][:, 0:8], in0=s["g2"][:, 24:32],
                                         scalar1=s["g2"][:, 19:20], scalar2=None,
                                         op0=ALU.mult)
            self.dve(c_pr, ["g2"], ["g3"], tag=f"{tag}pr{at}")
            def c_t2(eng):
                return eng.max(out=s["g3"][:, 8:16], in_=s["g3"][:, 0:8])
            self.dve(c_t2, ["g3"], ["g3"], tag=f"{tag}t2_{at}")
            def c_tb(eng):
                return eng.tensor_scalar(out=s["g3"][:, 16:17], in0=s["g3"][:, 9:10],
                                         scalar1=-2.0, scalar2=None, op0=ALU.mult)
            self.dve(c_tb, ["g3"], ["g3"], tag=f"{tag}tb{at}")
            def c_sg(eng):
                return eng.activation(s["g3"][:, 24:32], s["g3"][:, 0:8], AF.Sigmoid,
                                      bias=s["g3"][:, 16:17], scale=2.0)
            self.act(c_sg, ["g3"], ["g3"], tag=f"{tag}sg{at}")
            def c_w(eng):
                return eng.tensor_tensor(out=s["g3"][:, 32:40], in0=s["g3"][:, 0:8],
                                         in1=s["g3"][:, 24:32], op=ALU.mult)
            self.dve(c_w, ["g3"], ["g3"], tag=f"{tag}w{at}")
            def c_ws(eng):
                return eng.tensor_reduce(out=s["g3"][:, 40:41], in_=s["g3"][:, 32:40],
                                         axis=AX.X, op=ALU.add)
            self.dve(c_ws, ["g3"], ["g3"], tag=f"{tag}ws{at}")
            def c_we(eng):
                return eng.tensor_scalar(out=s["g3"][:, 41:42], in0=s["g3"][:, 40:41],
                                         scalar1=1e-9, scalar2=None, op0=ALU.add)
            self.dve(c_we, ["g3"], ["g3"], tag=f"{tag}we{at}")
            def c_wr(eng):
                return eng.reciprocal(out=s["g3"][:, 42:43], in_=s["g3"][:, 41:42])
            self.dve(c_wr, ["g3"], ["g3"], tag=f"{tag}wr{at}")
            def c_wn(eng):
                return eng.tensor_scalar(out=s["gb"][:, 32:40], in0=s["g3"][:, 32:40],
                                         scalar1=s["g3"][:, 42:43], scalar2=None,
                                         op0=ALU.mult)
            self.dve(c_wn, ["g3"], ["gb"], tag=f"{tag}wn{at}")
            self.transp(p["TB"][0:8, 0:128], s["gb"][:, 32:40], ["gb"], ["pTB"],
                        tag=f"{tag}wt{at}")
            self.copy("act", s["bsb"][0:8, at * 128:(at + 1) * 128],
                      p["TB"][0:8, 0:128], ["pTB"], ["wT"], tag=f"{tag}wte{at}")
        self.emit_dbg("f_wT", self.s["bsb"][0:8, 0:512], ["wT"])
        self.mm(p["X0"][0:32, :], s["smb"][0:8, 0:32], s["bsb"][0:8, 0:512],
                True, True, ["smb.L", "wT"], ["pX0"], tag=f"{tag}pm")
        def c_zp(eng):
            return eng.tensor_tensor(
                out=s["sqb"][0:32, 0:1024].rearrange("p (a e) -> p a e", e=2),
                in0=s["bfa"][0:32, 0:1024].rearrange("p (a e) -> p a e", e=2),
                in1=p["X0"][0:32, :].unsqueeze(2).broadcast_to((32, 512, 2)),
                op=ALU.mult)
        self.dve(c_zp, ["zd", "pX0"], ["zp"], tag=f"{tag}zp")
        self.emit_dbg("f_zp", self.s["sqb"][0:32, 0:1024], ["zp"])
        for oc in range(NFC):
            buf = oc % 2
            if wcache == "load":
                sc.dma("qs", s["wtile"][:, buf * 4096:(buf + 1) * 4096],
                       d["fWb"][:, oc * 4096:(oc + 1) * 4096],
                       reads=[f"fWb.{oc}"], writes=[f"wtile.{buf}"],
                       tag=f"{tag}W{oc}wl")
                wres = f"wtile.{buf}"
            else:
                wres = self.wcol_load("fW", oc * 128, 128, buf, f"{tag}W{oc}")
                sc.dma("qa", d["fWb"][:, oc * 4096:(oc + 1) * 4096],
                       s["wtile"][:, buf * 4096:(buf + 1) * 4096],
                       reads=[wres], writes=[f"fWb.{oc}"], tag=f"{tag}W{oc}ws")
            sc.dma("qs", s["fbst"][:, 0:128], d["fB"][:, oc * 128:(oc + 1) * 128],
                   reads=[], writes=["fbst"], tag=f"{tag}bl{oc}")
            def c_bc2(eng):
                return eng.mul(out=s["fbc"][:, 0:128], in_=s["fbst"][:, 0:128],
                               mul=0.5)
            self.act(c_bc2, ["fbst"], ["fbc"], tag=f"{tag}bc{oc}")
            sc.dma("qs", s["colsc"][:, 3:4],
                   d["fb"][oc * 128:(oc + 1) * 128].rearrange("(pp o) -> pp o", o=1),
                   reads=[], writes=["fb0"], tag=f"{tag}fb{oc}")
            bi = oc % 2
            xs = s["xsb"][:, bi * 1024:(bi + 1) * 1024]
            sc.dma("qa", xs, src[oc * 128:(oc + 1) * 128, :],
                   reads=[src_name], writes=[f"xsb.{bi}"], tag=f"{tag}lr{oc}")
            for hh in range(2):
                pw = p["W0" if hh == 0 else "W1"]
                for fc in range(NFC):
                    lhs = s["wtile"][:, buf * 4096 + fc * 128:buf * 4096 + (fc + 1) * 128]
                    rhs = s["big"][:, fc * 1024 + hh * 512:fc * 1024 + (hh + 1) * 512]
                    self.mm(pw[:], lhs, rhs, fc == 0, False,
                            [wres, f"big.{fc}"], [f"pW{hh}"], tag=f"{tag}hmm")
                self.mm(pw[:], s["fbc"][:, 0:128],
                        s["sqb"][0:32, hh * 512:(hh + 1) * 512], False, True,
                        ["fbc", "zp"], [f"pW{hh}"], tag=f"{tag}dmm")
                ta = s["tmpa"][:, bi * 1024 + hh * 512:bi * 1024 + (hh + 1) * 512]
                def c_ev(eng, pw=pw, ta=ta):
                    return eng.activation(ta, pw[:], AF.Identity,
                                          bias=s["colsc"][:, 3:4], scale=1.0)
                self.act(c_ev, [f"pW{hh}", "fb0"], [f"tmpa.{bi}.{hh}"],
                         tag=f"{tag}he{oc}{hh}")
                tb = s["tmpb"][:, bi * 1024 + hh * 512:bi * 1024 + (hh + 1) * 512]
                def c_ad(eng, ta=ta, xs=xs, tb=tb, hh=hh):
                    return eng.tensor_tensor(out=tb, in0=ta,
                                             in1=xs[:, hh * 512:(hh + 1) * 512],
                                             op=ALU.add)
                self.dve(c_ad, [f"tmpa.{bi}.{hh}", f"xsb.{bi}"],
                         [f"tmpb.{bi}.{hh}"], tag=f"{tag}ha{oc}{hh}")
            sc.dma("qs", dst[oc * 128:(oc + 1) * 128, :],
                   s["tmpb"][:, bi * 1024:(bi + 1) * 1024],
                   reads=[f"tmpb.{bi}.0", f"tmpb.{bi}.1"],
                   writes=[dst_name, f"{dst_name}.{oc}"], tag=f"{tag}hs{oc}")

    def emit_dbg(self, name, src_ap, reads):
        if name in self.dbg:
            self.sc.dma("qs", self.dbg[name], src_ap, reads=reads,
                        writes=[f"dbgo.{name}"], tag=f"dbg_{name}")

    def finish(self, last):
        def c_fin(eng):
            return eng.nop()
        reads = [last] + [f"{last}.{oc}" for oc in range(NFC)] + \
            [f"dbgo.{n}" for n in self.dbg]
        self.sc.add("dve", c_fin, reads, [], tag="final")

    def build_all(self, upto=4):
        self.load_consts()
        last = "xd"
        if upto >= 1:
            self.ln("xd", "L1n")
            self.emit_dbg("xhat1", self.s["big"][:],
                          [f"big.{i}" for i in range(NFC)])
            self.qkv("a", "L1p")
            self.emit_dbg("qrope1", self.d["qrope"], ["qrope"])
            self.emit_dbg("kvcon1", self.d["kvcon"], ["kvconK", "kvconV"])
            self.gather("L1g")
            self.attn("a", "L1a")
            self.emit_dbg("odall1", self.s["big"][:],
                          [f"big.{i}" for i in range(NFC)])
            self.wo("a", "xd", "xdA" if upto > 1 else "out", "L1o")
            last = "xdA" if upto > 1 else "out"
        if upto >= 2:
            self.ln(last, "L2n")
            self.ffn(last, "xdB" if upto > 2 else "out", "L2f")
            last = "xdB" if upto > 2 else "out"
        if upto >= 3:
            self.ln(last, "L3n")
            self.qkv("b", "L3p")
            self.emit_dbg("qrope3", self.d["qrope"], ["qrope"])
            self.emit_dbg("kvcon3", self.d["kvcon"], ["kvconK", "kvconV"])
            self.emit_dbg("negsM3", self.s["colsc"][:, 0:1], ["negsM"])
            self.gather("L3g")
            self.attn("b", "L3a")
            self.emit_dbg("odall3", self.s["big"][:],
                          [f"big.{i}" for i in range(NFC)])
            self.wo("b", last, "xdA" if upto > 3 else "out", "L3o")
            last = "xdA" if upto > 3 else "out"
        if upto >= 4:
            self.ln(last, "L4n")
            self.ffn(last, "out", "L4f", wcache="load")
            last = "out"
        self.finish(last)
        self.sc.finalize()

    def emit(self):
        nc, sc = self.nc, self.sc
        sem_cms = [nc.semaphore(f"sem_{s}_{k}") for (s, k) in sc.sem_keys]
        sems = {}
        for cm, key in zip(sem_cms, sc.sem_keys):
            sems[key] = cm.__enter__()
        self._cm += sem_cms
        blk = nc.Block()
        self._cm.append(blk)
        block = blk.__enter__()

        @block.tensor
        def _(eng):
            sc.replay("pe", eng, sems)

        @block.scalar
        def _(eng):
            sc.replay("act", eng, sems)

        @block.vector
        def _(eng):
            sc.replay("dve", eng, sems)

        @block.gpsimd
        def _(eng):
            sc.replay("gp", eng, sems)

        @block.sync
        def _(eng):
            sc.replay("qs", eng, sems)
        return self.nc



def _prep_inputs(inputs, core, consts=None, tables=None):
    b, r = core // 4, core % 4
    x = inputs["L0_attn_a_x"]
    xs = np.ascontiguousarray(
        np.asarray(x[b], np.float32)[r * 1024:(r + 1) * 1024, :].T)
    m = dict(
        xd=xs,
        aWq=np.asarray(inputs["L0_attn_a_Wq"], np.float32),
        aWk=np.asarray(inputs["L0_attn_a_Wk"], np.float32),
        aWv=np.asarray(inputs["L0_attn_a_Wv"], np.float32),
        aWo=np.asarray(inputs["L0_attn_a_Wo"], np.float32),
        abo=np.asarray(inputs["L0_attn_a_bo"], np.float32),
        bWq=np.ascontiguousarray(np.asarray(inputs["L0_attn_b_wq"], np.float32).T),
        bWk=np.ascontiguousarray(np.asarray(inputs["L0_attn_b_wk"], np.float32).T),
        bWv=np.ascontiguousarray(np.asarray(inputs["L0_attn_b_wv"], np.float32).T),
        bWo=np.ascontiguousarray(np.asarray(inputs["L0_attn_b_wo"], np.float32).T),
        fW=np.ascontiguousarray(np.asarray(inputs["L0_ffn_W"], np.float32).T),
        fb=np.asarray(inputs["L0_ffn_b"], np.float32),
        fA=np.asarray(inputs["L0_ffn_A"], np.float32),
        fB=np.asarray(inputs["L0_ffn_Bmat"], np.float32),
        fL=np.asarray(inputs["L0_ffn_LiMEs"], np.float32),
    )
    cst = consts if consts is not None else host_constants()
    tbl = tables[core] if tables is not None else host_core_tables(core)
    for k, v in cst.items():
        m[k] = np.asarray(v)
    for k, v in tbl.items():
        m[k] = np.asarray(v)
    return m


_BUILD_CACHE = {}


def get_program(upto=4, dbg=()):
    key = (upto, tuple(x[0] for x in dbg))
    if key not in _BUILD_CACHE:
        kb = KB(dbg=dbg)
        kb.build_all(upto=upto)
        nc = kb.emit()
        _BUILD_CACHE[key] = (nc, kb)
    return _BUILD_CACHE[key]


def kernel(**inputs):
    nc, kb = get_program()
    consts = host_constants()
    tables = {c: host_core_tables(c) for c in range(8)}
    in_maps = [_prep_inputs(inputs, core, consts, tables) for core in range(8)]
    res = run_bass_kernel_spmd(nc, in_maps, core_ids=list(range(8)))
    out = np.empty((B, T, C), np.float32)
    for core in range(8):
        b, r = core // 4, core % 4
        out[b, r * 1024:(r + 1) * 1024, :] = res.results[core]["out"].T
    return out
